# version 07
# baseline (speedup 1.0000x reference)
import numpy as np
from contextlib import ExitStack

import concourse.bass as bass
import concourse.mybir as mybir
from concourse.bass_utils import run_bass_kernel_spmd

F32 = mybir.dt.float32
BF16 = mybir.dt.bfloat16
AF = mybir.ActivationFunctionType
ALU = mybir.AluOpType
AX = mybir.AxisListType

STAGE = 6

D = 2048
DC = 16
TPC = 1024
TB = 512
NTB = 2
NE = 20
I_E = 1408
IC = 11
NH = 16
DH = 128
R = 8
SCALE = float(1.0 / np.sqrt(np.float32(DH)))

KV_RANK_ELEMS = 2 * D * TPC
K_OFF = 0
V_OFF = D * TPC


class Sched:

    def __init__(self, sems):
        self.streams = {k: [] for k in ("tensor", "vector", "scalar", "gpsimd", "sync")}
        self.cnt = {k: 0 for k in sems}
        self.sems = sems
        self.last_wait = {e: {} for e in self.streams}

    def op(self, eng, fn, wait=(), inc=None, inc_n=1):
        lw = self.last_wait[eng]
        w = []
        for sm, v in wait:
            if v > 0 and lw.get(sm, 0) < v:
                w.append((sm, v))
                lw[sm] = v
        ret = None
        if inc is not None:
            self.cnt[inc] += inc_n
            ret = self.cnt[inc]
        sems = self.sems
        wt = tuple(w)

        def thunk(e, _fn=fn, _wt=wt, _inc=inc, _n=inc_n):
            for sm, v in _wt:
                e.wait_ge(sems[sm], v)
            ins = _fn(e)
            if _inc is not None:
                ins.then_inc(sems[_inc], _n)
        self.streams[eng].append(thunk)
        return ret

    def wait(self, eng, wait):
        lw = self.last_wait[eng]
        sems = self.sems
        for sm, v in wait:
            if v > 0 and lw.get(sm, 0) < v:
                lw[sm] = v

                def thunk(e, _s=sm, _v=v):
                    e.wait_ge(sems[_s], _v)
                self.streams[eng].append(thunk)


def build_program(stage=STAGE):
    nc = bass.Bass(target_bir_lowering=False, debug=False,
                   detect_race_conditions=False)

    def din(name, shape):
        return nc.dram_tensor(name, shape, F32, kind="ExternalInput").ap()

    x_in = din("x_fm", [D, TPC])
    wq_in = din("wq_t", [NH, 128, DC * 128])
    wk_in = din("wk_t", [NH, 128, DC * 128])
    wv_in = din("wv_t", [NH, 128, DC * 128])
    wo_in = din("wo_t", [D, D])
    rt_in = din("router_t", [D, 16])
    se_in = din("se_t", [D, 1])
    gw_in = din("gw_all", [NE, IC, 128, DC * 128])
    uw_in = din("uw_all", [NE, IC, 128, DC * 128])
    dw_in = din("dw_all", [NE, 2, 128, IC * 1024])
    mk_in = din("masks", [R + 1, 128, 128])

    out_t = nc.dram_tensor("out", [D, TPC], F32, kind="ExternalOutput")
    out_ap = out_t.ap()

    q_scr = nc.dram_tensor("q_scr", [D, TPC], BF16).ap()
    NKV = D * TPC
    cck_in_t = nc.dram_tensor("cck_in", [NKV], BF16)
    ccv_in_t = nc.dram_tensor("ccv_in", [NKV], BF16)
    cck_out_t = nc.dram_tensor("cck_out", [R * NKV], BF16, addr_space="Shared")
    ccv_out_t = nc.dram_tensor("ccv_out", [R * NKV], BF16, addr_space="Shared")

    ccin_k = cck_in_t.ap().rearrange("(a b) -> a b", b=TPC)
    ccin_v4 = ccv_in_t.ap().rearrange("(h p a k) -> h p a k", h=NH, p=128, a=8)

    def rank_k(r):
        return cck_out_t.ap()[r * NKV:(r + 1) * NKV].rearrange("(a b) -> a b", b=TPC)

    def rank_v(r):
        return ccv_out_t.ap()[r * NKV:(r + 1) * NKV].rearrange(
            "(h p a k) -> h p a k", h=NH, p=128, a=8)

    RSZ = 211200
    rsv = nc.alloc_sbuf_tensor("rsv_manual", [128, RSZ], mybir.dt.uint8)
    SB = nc.lookup_mloc(rsv).addr
    off = SB

    def at(name, shape, dtype, o):
        return nc.alloc_sbuf_tensor_at(name, shape, dtype, offset=o)

    x_res = at("x_res", [128, DC * TPC], F32, off); off += 65536
    xln = at("xln", [128, DC * TPC], BF16, off); off += 32768
    dwA = at("dwA", [128, IC * 1024], BF16, off); off += 22528
    dwB = at("dwB", [128, IC * 1024], BF16, off); off += 22528
    gwb = [at(f"gw{j}", [128, DC * 128], BF16, off + 4096 * j) for j in range(2)]
    off += 8192
    uwb = [at(f"uw{j}", [128, DC * 128], BF16, off + 4096 * j) for j in range(2)]
    Pxt = [nc.alloc_sbuf_tensor_at(f"Pst{j+2}", [128, TB], BF16, offset=off + 4096 * j + 2048)
           for j in range(2)]
    off += 8192

    U = off; off += 32768
    Hb = [at("H0", [128, IC * TB], BF16, U),
          at("H1", [128, IC * TB], BF16, U + 11264)]
    silu_sb = [at(f"silu{j}", [128, TB], BF16, U + 22528 + 1024 * j) for j in range(2)]
    tmp_sb = [at(f"tmp{j}", [128, TB], BF16, U + 24576 + 1024 * j) for j in range(2)]
    wqb = [at(f"wqv{j}", [128, DC * 128], BF16, U + 4096 * j) for j in range(2)]
    wkb = [at(f"wkv{j}", [128, DC * 128], BF16, U + 8192 + 4096 * j) for j in range(2)]
    wvb = [at(f"wvv{j}", [128, DC * 128], BF16, U + 16384 + 4096 * j) for j in range(2)]
    vstage = at("vstage", [128, 128], BF16, U + 24576)
    kqstage = [at(f"kqs{j}", [128, TB], BF16, U + 24832 + 1024 * j) for j in range(4)]
    z_qb = at("z_qb", [128, NH * TB], BF16, U)
    Kst = [at(f"Kst{j}", [128, TPC], BF16, U + 16384 + 2048 * j) for j in range(2)]
    Vst = [at(f"Vst{j}", [128, 8 * 128], BF16, U + 20480 + 2048 * j) for j in range(2)]
    Pst = [at(f"Pst{j}", [128, TB], BF16, U + 24576 + 1024 * j) for j in range(2)]
    Qt = [at(f"Qt{j}", [128, TB], BF16, U + 26624 + 1024 * j) for j in range(2)]
    wob = [at(f"wob{j}", [128, TB], BF16, U + 28672 + 1024 * j) for j in range(2)]
    linv_b = at("linv_b", [128, TB], F32, U + 30720)

    USP = U + 26624
    wden_all = at("wden_all", [128, 8 * 16], BF16, USP)
    wrow_bf = at("wrow_bf", [1, TB], BF16, USP + 256)
    wb_sb = [at(f"wb{j}", [128, TB], BF16, USP + 1280 + 1024 * j) for j in range(2)]
    sig_sb = [at(f"sig{j}", [128, TB], BF16, USP + 3328 + 1024 * j) for j in range(2)]
    probs = at("probs", [128, 16], F32, USP + 5376)
    maxes = at("maxes", [128, 8], F32, USP + 5440)
    wmask = at("wmask", [128, 16], F32, USP + 5472)

    ab_a = at("ab_a", [128, TB], F32, off); off += 2048
    ab_b = at("ab_b", [128, TB], F32, off); off += 2048
    sq_sb = [at(f"sq{j}", [128, TB], BF16, off + 1024 * j) for j in range(2)]; off += 2048
    masks_bf = at("masks_bf", [128, (R + 1) * 128], BF16, off); off += 2304
    router_bf = at("router_bf", [128, DC * 16], BF16, off); off += 512
    se_bf = at("se_bf", [128, DC], BF16, off); off += 32
    ones_col_bf = at("ones_col_bf", [128, 1], BF16, off); off += 32
    ones_row_bf = at("ones_row_bf", [1, 128], BF16, off); off += 256
    ones_row_f = at("ones_row_f", [1, 128], F32, off); off += 512
    row_m_t = at("row_m", [1, TB], F32, off); off += 2048
    row_m = row_m_t[0:1, :]
    row_a = at("row_a", [1, TB], F32, off); off += 2048
    row_b = at("row_b", [1, TB], F32, off); off += 2048
    row_l_off = off
    row_l = at("row_l", [1, TB], F32, off); off += 2048
    row_sg = at("row_sg", [1, TB], BF16, row_l_off)
    row_s_t = at("row_s", [1, TB], F32, row_l_off)
    row_s = row_s_t[0:1, :]
    rmax_neg = at("rmax_neg", [128, 1], F32, off); off += 32
    rsum = at("rsum", [128, 1], F32, off); off += 32
    rsum2 = at("rsum2", [128, 1], F32, off); off += 32
    assert off <= SB + RSZ, f"SBUF overflow: {off} > {SB + RSZ}"

    ctx = ExitStack()
    ps = [ctx.enter_context(nc.psum_tensor(f"ps{j}", [128, TB], F32)) for j in range(8)]

    semnames = ["sX", "sI", "sT", "sV", "sS", "sC", "sO",
                "sG0", "sG1", "sD0", "sD1", "sQW0", "sQW1",
                "sWo0", "sWo1", "sK0", "sK1", "sV0", "sV1", "sQ0", "sQ1",
                "sE0", "sE1", "sE2", "sE3", "sEV"]
    sems = {n: ctx.enter_context(nc.semaphore(n)) for n in semnames}
    s = Sched(sems)

    def t_mm(out, lhsT, rhs, start, stop, wait=(), inc=None):
        return s.op("tensor",
                    lambda e, o=out, l=lhsT, r=rhs, st=start, sp=stop:
                    e.matmul(o, l, r, start=st, stop=sp, skip_group_check=True),
                    wait=wait, inc=inc)

    bank_free = [("sT", 0)] * 8

    def chain(bank, pairs, wait=(), out=None, inc="sT"):
        o = ps[bank][:, :] if out is None else out
        n = len(pairs)
        w = list(wait) + [bank_free[bank]]
        cnt = None
        for j, (l, r) in enumerate(pairs):
            cnt = t_mm(o, l, r, j == 0, j == n - 1,
                       wait=w if j == 0 else (), inc=inc if j == n - 1 else None)
        return cnt

    def dma(eng, sem, dst, src, wait=(), slow=False):
        cnt = s.op(eng, lambda e, d=dst, sr=src, sl=slow: e.dma_start(
            out=d, in_=sr, allow_slow_non_contiguous=sl),
            wait=wait, inc=sem, inc_n=16)
        return (sem, cnt)

    def g_dma(sem, dst, src, wait=(), slow=False):
        return dma("gpsimd", sem, dst, src, wait, slow)

    def a_dma(sem, dst, src, wait=()):
        return dma("scalar", sem, dst, src, wait)

    def k_dma(sem, dst, src, wait=()):
        return dma("sync", sem, dst, src, wait)

    x_ld = k_dma("sX", x_res[:, :].rearrange("p (c t) -> p c t", t=TPC),
                 x_in.rearrange("(c p) t -> p c t", p=128))
    g_dma("sI", masks_bf[:, :].rearrange("p (r q) -> p r q", q=128),
          mk_in.rearrange("r p q -> p r q"))
    g_dma("sI", router_bf[:, :].rearrange("p (c e) -> p c e", e=16),
          rt_in.rearrange("(c p) e -> p c e", p=128))
    init_ld = g_dma("sI", se_bf[:, :], se_in.rearrange("(c p) e -> p (c e)", p=128),
                    slow=True)
    mk_ld = rt_ld = se_ld = init_ld

    s.op("vector", lambda e: e.memset(ones_col_bf[:, :], 1.0))
    s.op("vector", lambda e: e.memset(ones_row_bf[:, :], 1.0))
    s.op("vector", lambda e: e.memset(ones_row_f[:, :], 1.0))
    const_cnt = s.op("vector", lambda e: e.memset(rmax_neg[:, :], 0.0), inc="sV")

    xln_ready = {0: 0, 1: 0}
    st = {
        "slab_free": {0: 0, 1: 0},
        "dw_free": {0: 0, 1: 0},
        "H_free": {0: 0, 1: 0},
        "wqkv_free": {0: 0, 1: 0},
        "K_free": {0: 0, 1: 0},
        "V_free": {0: 0, 1: 0},
        "Qt_free": {0: 0, 1: 0},
        "wo_free": {0: 0, 1: 0},
        "P_free": {j: 0 for j in range(4)},
        "kq_free": {j: (f"sE{j}", 0) for j in range(4)},
        "vstage_free": ("sEV", 0),
        "sig_ready": {0: 0, 1: 0},
        "wdT_ready": 0,
        "wb_ready": {0: 0, 1: 0},
        "row_a_free": 0,
        "row_b_free": 0,
        "row_sg_free": 0,
        "row_l_free": 0,
        "row_s_free": 0,
    }

    def xs(c, toff, n=TB):
        return xln[:, c * TPC + toff: c * TPC + toff + n]

    def emit_ln(tb, x_wait):
        tb_off = tb * TB
        cast_cnt = []
        for c in range(DC):
            w = list(x_wait) if c == 0 else ()
            cc = s.op("scalar",
                      lambda e, c=c: e.copy(out=xs(c, tb_off), in_=x_res[:, c * TPC + tb_off: c * TPC + tb_off + TB]),
                      wait=w, inc="sS")
            cast_cnt.append(cc)

        sq_cnt = [0] * DC
        sqmm_cnt = [0] * DC
        for c in range(DC):
            w = [("sS", cast_cnt[c])]
            if c >= 2:
                w.append(("sT", sqmm_cnt[c - 2]))
            sq_cnt[c] = s.op("vector",
                             lambda e, c=c: e.tensor_mul(out=sq_sb[c % 2][:, :],
                                                         in0=xs(c, tb_off), in1=xs(c, tb_off)),
                             wait=w, inc="sV")
            w = [("sS", cast_cnt[c]), ("sV", const_cnt)]
            if c == 0:
                w += [bank_free[6], bank_free[7]]
            t_mm(ps[6][0:1, :], ones_col_bf[:, :], xs(c, tb_off),
                 c == 0, c == DC - 1, wait=w, inc="sT" if c == DC - 1 else None)
            sqmm_cnt[c] = t_mm(ps[7][0:1, :], ones_col_bf[:, :], sq_sb[c % 2][:, :],
                               c == 0, c == DC - 1,
                               wait=[("sV", sq_cnt[c])], inc="sT")
        stats_mm = sqmm_cnt[DC - 1]

        s.op("vector", lambda e: e.tensor_scalar_mul(row_m, ps[6][0:1, :], 1.0 / D),
             wait=[("sT", stats_mm)])
        s.op("vector", lambda e: e.tensor_scalar_mul(row_s, ps[7][0:1, :], 1.0 / D),
             wait=[("sT", st["row_l_free"]), ("sT", st["row_sg_free"])])
        s.op("vector", lambda e: e.tensor_mul(row_a[:, :], row_m, row_m),
             wait=[("sT", st["row_a_free"])])
        s.op("vector", lambda e: e.tensor_sub(row_s, row_s, row_a[:, :]))
        vcnt = s.op("vector", lambda e: e.tensor_scalar_add(row_s, row_s, 1e-5),
                    inc="sV")
        bank_free[6] = ("sV", vcnt)
        bank_free[7] = ("sV", vcnt)
        s.op("scalar", lambda e: e.activation(out=row_b[:, :], in_=row_s,
                                              func=AF.Ln),
             wait=[("sV", vcnt), ("sT", st["row_b_free"])])
        scnt = s.op("scalar", lambda e: e.activation(out=row_a[:, :], in_=row_b[:, :],
                                                     func=AF.Exp, scale=-0.5),
                    wait=[("sT", st["row_a_free"])], inc="sS")
        st["row_s_free"] = scnt
        s.op("vector", lambda e: e.tensor_mul(row_b[:, :], row_m, row_a[:, :]),
             wait=[("sS", scnt)])
        bcnt = s.op("vector", lambda e: e.tensor_scalar_mul(row_b[:, :], row_b[:, :], -1.0),
                    inc="sV")
        am = t_mm(ps[4][:, :], ones_row_f[:, :], row_a[:, :], True, True,
                  wait=[("sS", scnt), bank_free[4]], inc="sT")
        bm = t_mm(ps[5][:, :], ones_row_f[:, :], row_b[:, :], True, True,
                  wait=[("sV", bcnt), bank_free[5]], inc="sT")
        st["row_a_free"] = am
        st["row_b_free"] = bm
        ea = s.op("vector", lambda e: e.tensor_copy(out=ab_a[:, :], in_=ps[4][:, :]),
                  wait=[("sT", am)], inc="sV")
        eb = s.op("vector", lambda e: e.tensor_copy(out=ab_b[:, :], in_=ps[5][:, :]),
                  wait=[("sT", bm)], inc="sV")
        bank_free[4] = ("sV", ea)
        bank_free[5] = ("sV", eb)
        last = 0
        for c in range(DC):
            w = [("sV", eb), bank_free[3]] if c == 0 else ()
            s.op("vector", lambda e, c=c: e.tensor_mul(
                ps[3][:, :], x_res[:, c * TPC + tb_off: c * TPC + tb_off + TB], ab_a[:, :]),
                wait=w)
            last = s.op("vector", lambda e, c=c: e.tensor_add(
                xs(c, tb_off), ps[3][:, :], ab_b[:, :]), inc="sV")
        bank_free[3] = ("sV", last)
        xln_ready[tb] = last

    def emit_routing():
        exp_cnt = 0
        wdc = 0
        for tc in range(8):
            tb = tc // 4
            col = tc * 128
            w = [("sV", xln_ready[1]), rt_ld, bank_free[4]]
            if tc > 0:
                w.append(("sS", exp_cnt))
            mmc = None
            for j in range(DC):
                mmc = t_mm(ps[4][:, 0:16], xs(j, col, 128), router_bf[:, j * 16:(j + 1) * 16],
                           j == 0, j == DC - 1, wait=w if j == 0 else (),
                           inc="sT" if j == DC - 1 else None)
            rm = s.op("vector", lambda e: e.reduce_max(rmax_neg[:, :], ps[4][:, 0:16],
                                                       axis=AX.X, negate=True),
                      wait=[("sT", mmc)], inc="sV")
            eu = s.op("scalar", lambda e: e.activation(out=probs[:, :], in_=ps[4][:, 0:16],
                                                       func=AF.Exp, bias=rmax_neg[:, 0:1]),
                      wait=[("sV", rm)], inc="sS")
            rs = s.op("vector", lambda e: e.reduce_sum(rsum[:, :], probs[:, :], axis=AX.X),
                      wait=[("sS", eu)], inc="sV")
            s.op("scalar", lambda e: e.activation(out=rsum2[:, :], in_=rsum[:, :],
                                                  func=AF.Ln), wait=[("sV", rs)])
            s.op("scalar", lambda e: e.activation(out=rsum2[:, :], in_=rsum2[:, :],
                                                  func=AF.Exp, scale=-1.0))
            exp_cnt = s.op("scalar", lambda e: e.activation(out=probs[:, :], in_=probs[:, :],
                                                            func=AF.Identity,
                                                            scale=rsum2[:, 0:1]),
                           inc="sS")
            c3 = s.op("vector", lambda e: e.max(out=maxes[:, :], in_=probs[:, :]),
                      wait=[("sS", exp_cnt)], inc="sV")
            c4 = s.op("vector", lambda e: e.tensor_scalar(out=wmask[:, :], in0=probs[:, :],
                                                          scalar1=maxes[:, 3:4], scalar2=None,
                                                          op0=ALU.is_ge),
                      wait=[("sV", c3)], inc="sV")
            c5 = s.op("vector", lambda e: e.tensor_mul(probs[:, :], probs[:, :], wmask[:, :]),
                      wait=[("sV", c4)], inc="sV")
            s.wait("vector", [("sV", c5)])
            wdc = s.op("vector", lambda e, tc=tc: e.tensor_copy(
                out=wden_all[:, tc * 16:(tc + 1) * 16], in_=probs[:, :]),
                wait=[("sT", s.cnt["sT"])] if tc == 0 else (), inc="sV")
        bank_free[4] = ("sS", exp_cnt)
        st["wdT_ready"] = wdc

        for tb in range(NTB):
            tb_off = tb * TB
            pairs = [(se_bf[:, c:c + 1], xs(c, tb_off)) for c in range(DC)]
            mmc = chain(6, pairs, wait=[("sV", xln_ready[tb]), se_ld],
                        out=ps[6][0:1, :])
            sg = s.op("scalar", lambda e: e.activation(out=row_sg[:, :], in_=ps[6][0:1, :],
                                                       func=AF.Sigmoid),
                      wait=[("sT", mmc), ("sT", st["row_sg_free"])], inc="sS")
            bank_free[6] = ("sS", sg)
            bm = t_mm(ps[5][:, :], ones_row_bf[:, :], row_sg[:, :], True, True,
                      wait=[("sS", sg), bank_free[5]], inc="sT")
            st["row_sg_free"] = bm
            sv = s.op("vector", lambda e, tb=tb: e.tensor_copy(out=sig_sb[tb][:, :],
                                                               in_=ps[5][:, :]),
                      wait=[("sT", bm)], inc="sV")
            bank_free[5] = ("sV", sv)
            st["sig_ready"][tb] = sv

    def emit_expert(e_idx):
        moe = e_idx < 16
        h_free_prev = dict(st["H_free"])

        slab_ld = [0] * IC

        def load_slab(i, free_cnt):
            par = i % 2
            sem = f"sG{par}"
            g_dma(sem, gwb[par][:, :], gw_in[e_idx, i], wait=[("sT", free_cnt)])
            slab_ld[i] = g_dma(sem, uwb[par][:, :], uw_in[e_idx, i])

        load_slab(0, st["slab_free"][0])
        load_slab(1, st["slab_free"][1])
        dwa_ld = g_dma("sD0", dwA[:, :], dw_in[e_idx, 0],
                       wait=[("sT", st["dw_free"][0])])
        dwb_ld = g_dma("sD1", dwB[:, :], dw_in[e_idx, 1],
                       wait=[("sT", st["dw_free"][1])])

        if moe:
            ident = masks_bf[:, R * 128:(R + 1) * 128]
            for tb in range(NTB):
                tc0 = 4 * tb
                tmc = 0
                for j in range(4):
                    w = [("sV", st["wdT_ready"]), mk_ld, bank_free[5]] if j == 0 else ()
                    tmc = t_mm(ps[5][0:1, 128 * j:128 * (j + 1)],
                               wden_all[:, (tc0 + j) * 16 + e_idx:(tc0 + j) * 16 + e_idx + 1],
                               ident, True, True, wait=w,
                               inc="sT" if j == 3 else None)
                re_ = s.op("vector", lambda e: e.tensor_copy(out=wrow_bf[:, :],
                                                             in_=ps[5][0:1, :]),
                           wait=[("sT", tmc)], inc="sV")
                bm = t_mm(ps[5][:, :], ones_row_bf[:, :], wrow_bf[:, :],
                          True, True, wait=[("sV", re_)], inc="sT")
                wv = s.op("vector", lambda e, tb=tb: e.tensor_copy(out=wb_sb[tb][:, :],
                                                                   in_=ps[5][:, :]),
                          wait=[("sT", bm)], inc="sV")
                bank_free[5] = ("sV", wv)
                st["wb_ready"][tb] = wv
            wb_use = wb_sb
            wb_wait = dict(st["wb_ready"])
        else:
            wb_use = sig_sb
            wb_wait = dict(st["sig_ready"])

        h_cnt = {0: 0, 1: 0}
        first_h = {0: True, 1: True}
        for i in range(IC):
            par = i % 2
            wslab = [slab_ld[i]]
            for tb in range(NTB):
                gb, ub = (0, 1) if tb == 0 else (2, 3)
                pairs_g = [(gwb[par][:, 128 * c:128 * (c + 1)], xs(c, tb * TB))
                           for c in range(DC)]
                gc = chain(gb, pairs_g, wait=wslab + [("sV", xln_ready[tb])])
                pairs_u = [(uwb[par][:, 128 * c:128 * (c + 1)], xs(c, tb * TB))
                           for c in range(DC)]
                uc = chain(ub, pairs_u, wait=wslab)
                if tb == NTB - 1:
                    st["slab_free"][par] = uc
                sl = s.op("scalar", lambda e, par=par, gb=gb: e.activation(
                    out=silu_sb[par][:, :], in_=ps[gb][:, :], func=AF.Silu),
                    wait=[("sT", gc)], inc="sS")
                bank_free[gb] = ("sS", sl)
                tm = s.op("vector", lambda e, par=par, ub=ub, tb=tb: e.tensor_mul(
                    out=tmp_sb[par][:, :], in0=ps[ub][:, :], in1=wb_use[tb][:, :]),
                    wait=[("sT", uc), ("sV", wb_wait[tb])], inc="sV")
                bank_free[ub] = ("sV", tm)
                hw = [("sS", sl)]
                if first_h[tb]:
                    hw.append(("sT", h_free_prev[tb]))
                    first_h[tb] = False
                h_cnt[tb] = s.op("vector", lambda e, par=par, tb=tb, i=i: e.tensor_mul(
                    out=Hb[tb][:, i * TB:(i + 1) * TB],
                    in0=silu_sb[par][:, :], in1=tmp_sb[par][:, :]),
                    wait=hw, inc="sV")
            if i + 2 < IC:
                load_slab(i + 2, st["slab_free"][(i + 2) % 2])

        for tb in range(NTB):
            for dout in range(DC):
                bank = dout % 2
                dwbuf, dw_ld, half = (dwA, dwa_ld, 0) if dout < 8 else (dwB, dwb_ld, 1)
                dcol = 128 * (dout - 8 * half)
                pairs = [(dwbuf[:, c * 1024 + dcol: c * 1024 + dcol + 128],
                          Hb[tb][:, c * TB:(c + 1) * TB]) for c in range(IC)]
                dc = chain(bank, pairs, wait=[dw_ld, ("sV", h_cnt[tb])])
                if dout == 7:
                    if tb == NTB - 1:
                        st["dw_free"][0] = dc
                elif dout == DC - 1:
                    st["H_free"][tb] = dc
                    if tb == NTB - 1:
                        st["dw_free"][1] = dc
                ev = s.op("vector", lambda e, tb=tb, dout=dout, bank=bank:
                          e.scalar_tensor_tensor(
                              out=x_res[:, dout * TPC + tb * TB: dout * TPC + tb * TB + TB],
                              in0=ps[bank][:, :], scalar=0.5,
                              in1=x_res[:, dout * TPC + tb * TB: dout * TPC + tb * TB + TB],
                              op0=ALU.mult, op1=ALU.add),
                          wait=[("sT", dc)], inc="sV")
                bank_free[bank] = ("sV", ev)

    def emit_qkv():
        u_free = [("sT", s.cnt["sT"]), ("sV", s.cnt["sV"])]
        kq_slot = [0]

        def export(psbank, mmc, dst):
            sl = kq_slot[0] % 4
            kq_slot[0] += 1
            ev = s.op("vector", lambda e, b=psbank, sl=sl: e.tensor_copy(
                out=kqstage[sl][:, :], in_=ps[b][:, :]),
                wait=[("sT", mmc), st["kq_free"][sl]], inc="sV")
            bank_free[psbank] = ("sV", ev)
            st["kq_free"][sl] = a_dma(f"sE{sl}", dst, kqstage[sl][:, :],
                                      wait=[("sV", ev)])

        for h in range(NH):
            par = h % 2
            sem = f"sQW{par}"
            w = [("sT", st["wqkv_free"][par])]
            if h == 0:
                w += u_free
            wld = g_dma(sem, wkb[par][:, :], wk_in[h], wait=w)
            last_mm = 0
            for tb in range(NTB):
                kb = tb
                pairs = [(wkb[par][:, 128 * c:128 * (c + 1)], xs(c, tb * TB))
                         for c in range(DC)]
                kc = chain(kb, pairs, wait=[wld, ("sV", xln_ready[tb])])
                last_mm = kc
                export(kb, kc, ccin_k[128 * h:128 * (h + 1), tb * TB:tb * TB + TB])
            st["wqkv_free"][par] = last_mm
        k_exp_waits = [(f"sE{j}", s.cnt[f"sE{j}"]) for j in range(4)]
        cck_cnt = s.op("gpsimd",
                       lambda e: e.collective_compute(
                           "AllGather", ALU.bypass,
                           replica_groups=[list(range(R))],
                           ins=[cck_in_t.ap().opt()],
                           outs=[cck_out_t.ap().opt()]),
                       wait=k_exp_waits, inc="sC")

        for h in range(NH):
            par = h % 2
            wld = g_dma(f"sQW{par}", wvb[par][:, :], wv_in[h],
                        wait=[("sT", st["wqkv_free"][par])])
            last_mm = 0
            for tc in range(8):
                vb = 4 + (tc % 2)
                toff = tc * 128
                pairs = [(xs(c, toff, 128), wvb[par][:, 128 * c:128 * (c + 1)])
                         for c in range(DC)]
                vc = chain(vb, pairs, wait=[wld], out=ps[vb][:, 0:128])
                last_mm = vc
                ev = s.op("vector", lambda e, vb=vb: e.tensor_copy(
                    out=vstage[:, :], in_=ps[vb][:, 0:128]),
                    wait=[("sT", vc), st["vstage_free"]], inc="sV")
                bank_free[vb] = ("sV", ev)
                st["vstage_free"] = a_dma(
                    "sEV", ccin_v4[h, :, tc, :], vstage[:, :], wait=[("sV", ev)])
            st["wqkv_free"][par] = last_mm
        ccv_cnt = s.op("gpsimd",
                       lambda e: e.collective_compute(
                           "AllGather", ALU.bypass,
                           replica_groups=[list(range(R))],
                           ins=[ccv_in_t.ap().opt()],
                           outs=[ccv_out_t.ap().opt()]),
                       wait=[("sEV", s.cnt["sEV"])], inc="sC")

        for h in range(NH):
            par = h % 2
            wld = g_dma(f"sQW{par}", wqb[par][:, :], wq_in[h],
                        wait=[("sT", st["wqkv_free"][par])])
            last_mm = 0
            for tb in range(NTB):
                qb_ = 2 + tb
                pairs = [(wqb[par][:, 128 * c:128 * (c + 1)], xs(c, tb * TB))
                         for c in range(DC)]
                qc = chain(qb_, pairs, wait=[wld])
                last_mm = qc
                export(qb_, qc, q_scr[128 * h:128 * (h + 1), tb * TB:tb * TB + TB])
            st["wqkv_free"][par] = last_mm
        st["export_done"] = [(f"sE{j}", s.cnt[f"sE{j}"]) for j in range(4)]
        return cck_cnt, ccv_cnt

    def emit_flash(qb, cc_cnt):
        cck_cnt, ccv_cnt = cc_cnt
        if qb == 0:
            tcur = s.cnt["sT"]
            for j in range(2):
                st["K_free"][j] = max(st["K_free"][j], tcur)
                st["V_free"][j] = max(st["V_free"][j], tcur)
                st["Qt_free"][j] = max(st["Qt_free"][j], tcur)
                st["wo_free"][j] = max(st["wo_free"][j], tcur)
        amax = 4 * qb + 4
        SBANKS = [0, 1, 3, 4]
        PD = 4
        Pfull = Pst + Pxt
        exp_free = {j: 0 for j in range(PD)}
        first_s = {j: True for j in range(PD)}
        for h in range(NH):
            qpar = h % 2
            qld = a_dma(f"sQ{qpar}", Qt[qpar][:, :],
                        q_scr[128 * h:128 * (h + 1), qb * TB:qb * TB + TB],
                        wait=[("sT", st["Qt_free"][qpar])])
            lastS = 0
            lastAV = 0
            tiles = [(r, a) for r in range(R) for a in range(amax)]
            nt = len(tiles)
            kvld = {}
            pend = {}
            AHEAD = 2

            def emit_S(j):
                nonlocal lastS
                r, a = tiles[j]
                rpar = r % 2
                if a == 0:
                    kld = k_dma(f"sK{rpar}", Kst[rpar][:, 0:amax * 128],
                                rank_k(r)[128 * h:128 * (h + 1), 0:amax * 128],
                                wait=[("sT", st["K_free"][rpar]), ("sC", cck_cnt)])
                    vld = k_dma(
                        f"sV{rpar}",
                        Vst[rpar][:, 0:amax * 128].rearrange("p (a k) -> p a k", k=128),
                        rank_v(r)[h, :, 0:amax, :],
                        wait=[("sT", st["V_free"][rpar]), ("sC", ccv_cnt)])
                    kvld[r] = (kld, vld)
                kld, vld = kvld[r]
                spar = j % PD
                sbank = SBANKS[spar]
                jc = a - 4 * qb
                c0 = 128 * jc if jc > 0 else 0
                w = [kld, qld, ("sS", exp_free[spar])]
                if first_s[spar]:
                    w.append(bank_free[sbank])
                    first_s[spar] = False
                smm = t_mm(ps[sbank][:, c0:TB], Kst[rpar][:, 128 * a:128 * (a + 1)],
                           Qt[qpar][:, c0:TB], True, True, wait=w, inc="sT")
                lastS = smm
                ex = s.op("scalar", lambda e, spar=spar, sbank=sbank, c0=c0: e.activation(
                    out=Pfull[spar][:, c0:TB], in_=ps[sbank][:, c0:TB], func=AF.Exp,
                    scale=SCALE),
                    wait=[("sT", smm), ("sT", st["P_free"][spar])], inc="sS")
                exp_free[spar] = ex
                p_wait = [("sS", ex)]
                if jc >= 0:
                    mv = s.op("vector", lambda e, spar=spar, c0=c0, r=r: e.tensor_mul(
                        out=Pfull[spar][:, c0:c0 + 128],
                        in0=Pfull[spar][:, c0:c0 + 128],
                        in1=masks_bf[:, 128 * r:128 * (r + 1)]),
                        wait=[("sS", ex), mk_ld], inc="sV")
                    p_wait = [("sV", mv)]
                pend[j] = (r, a, spar, c0, p_wait, vld, smm)

            def emit_AVl(j):
                nonlocal lastAV
                r, a, spar, c0, p_wait, vld, smm = pend.pop(j)
                rpar = r % 2
                startg = (j == 0)
                stopg = (j == nt - 1)
                w2 = list(p_wait) + [vld]
                if startg:
                    w2 += [bank_free[2], bank_free[6]]
                t_mm(ps[2][:, c0:TB], Vst[rpar][:, 128 * a:128 * (a + 1)],
                     Pfull[spar][:, c0:TB], startg, stopg, wait=w2)
                lastAV = t_mm(ps[6][0:1, c0:TB], ones_col_bf[:, :],
                              Pfull[spar][:, c0:TB], startg, stopg, inc="sT")
                st["P_free"][spar] = lastAV
                if a == amax - 1:
                    st["K_free"][rpar] = smm
                    st["V_free"][rpar] = lastAV

            for j in range(nt):
                emit_S(j)
                if j == min(AHEAD, nt - 1) and st.get("pend_epi"):
                    st["pend_epi"]()
                    st["pend_epi"] = None
                if j >= AHEAD:
                    emit_AVl(j - AHEAD)
            for j in range(nt - AHEAD, nt):
                emit_AVl(j)
            st["Qt_free"][qpar] = lastS
            for j in range(PD):
                bank_free[SBANKS[j]] = ("sS", exp_free[j])

            def epilogue(h=h, lastAV=lastAV):
                s.op("scalar", lambda e: e.activation(out=row_l[:, :], in_=ps[6][0:1, :],
                                                      func=AF.Ln),
                     wait=[("sT", lastAV), ("sT", st["row_l_free"]),
                           ("sS", st["row_s_free"])])
                rc = s.op("scalar", lambda e: e.activation(out=row_l[:, :], in_=row_l[:, :],
                                                           func=AF.Exp, scale=-1.0),
                          inc="sS")
                bank_free[6] = ("sS", rc)
                bm = t_mm(ps[5][:, :], ones_row_f[:, :], row_l[:, :], True, True,
                          wait=[("sS", rc), bank_free[5]], inc="sT")
                st["row_l_free"] = bm
                lb = s.op("vector", lambda e: e.tensor_copy(out=linv_b[:, :], in_=ps[5][:, :]),
                          wait=[("sT", bm)], inc="sV")
                bank_free[5] = ("sV", lb)
                zc = s.op("vector", lambda e, h=h: e.tensor_mul(
                    out=z_qb[:, h * TB:(h + 1) * TB], in0=ps[2][:, :], in1=linv_b[:, :]),
                    wait=[("sV", lb)], inc="sV")
                bank_free[2] = ("sV", zc)

            st["pend_epi"] = epilogue
        if st.get("pend_epi"):
            st["pend_epi"]()
            st["pend_epi"] = None
        z_done = s.cnt["sV"]

        for grp in range(4):
            wo_ld = [0] * NH

            def load_wo(f, free_cnt, extra=()):
                par = f % 2
                wo_ld[f] = g_dma(f"sWo{par}", wob[par][:, :],
                                 wo_in[128 * f:128 * (f + 1), grp * TB:grp * TB + TB],
                                 wait=[("sT", free_cnt)] + list(extra))

            ex0 = st["export_done"] if (qb == 0 and grp == 0) else ()
            load_wo(0, st["wo_free"][0], ex0)
            load_wo(1, st["wo_free"][1], ex0)
            ch = [0] * 4
            for f in range(NH):
                par = f % 2
                fcnt = 0
                for dd in range(4):
                    w = [wo_ld[f]]
                    if f == 0:
                        w += [bank_free[dd], ("sV", z_done)]
                    fcnt = t_mm(ps[dd][:, :], wob[par][:, 128 * dd:128 * (dd + 1)],
                                z_qb[:, f * TB:(f + 1) * TB],
                                f == 0, f == NH - 1, wait=w, inc="sT")
                    if f == NH - 1:
                        ch[dd] = fcnt
                if f + 2 < NH:
                    load_wo(f + 2, fcnt)
            st["wo_free"][0] = ch[3]
            st["wo_free"][1] = ch[3]
            for dd in range(4):
                dout = 4 * grp + dd
                ev = s.op("vector", lambda e, dout=dout, dd=dd: e.tensor_add(
                    out=x_res[:, dout * TPC + qb * TB: dout * TPC + qb * TB + TB],
                    in0=ps[dd][:, :],
                    in1=x_res[:, dout * TPC + qb * TB: dout * TPC + qb * TB + TB]),
                    wait=[("sT", ch[dd])], inc="sV")
                bank_free[dd] = ("sV", ev)

    emit_ln(0, [x_ld, ("sV", const_cnt)])
    emit_ln(1, [x_ld])
    if stage >= 2:
        emit_routing()
    if stage >= 3:
        for e_idx in range(NE):
            emit_expert(e_idx)
    ffn1_done = s.cnt["sV"]

    cc_cnt = 0
    if stage >= 4:
        emit_ln(0, [("sV", ffn1_done)])
        emit_ln(1, [("sV", ffn1_done)])
        cc_cnt = emit_qkv()
    if stage >= 5:
        emit_flash(0, cc_cnt)
        x2_q0 = s.cnt["sV"]
        if stage >= 6:
            emit_ln(0, [("sV", x2_q0)])
        emit_flash(1, cc_cnt)
        x2_q1 = s.cnt["sV"]
    if stage >= 6:
        emit_ln(1, [("sV", x2_q1)])
        emit_routing()
        st["H_free"][0] = max(st["H_free"][0], s.cnt["sT"])
        st["H_free"][1] = max(st["H_free"][1], s.cnt["sT"])
        st["slab_free"][0] = max(st["slab_free"][0], s.cnt["sT"])
        st["slab_free"][1] = max(st["slab_free"][1], s.cnt["sT"])
        for e_idx in range(NE):
            emit_expert(e_idx)

    x_done = s.cnt["sV"]
    k_dma("sO", out_ap.rearrange("(c p) t -> p c t", p=128),
          x_res[:, :].rearrange("p (c t) -> p c t", t=TPC),
          wait=[("sV", x_done)])

    if stage == 5:
        dbg_q = nc.dram_tensor("dbg_q", [D, TPC], BF16, kind="ExternalOutput").ap()
        dbg_cck = nc.dram_tensor("dbg_cck", [R * NKV], BF16, kind="ExternalOutput").ap()
        dbg_ccv = nc.dram_tensor("dbg_ccv", [R * NKV], BF16, kind="ExternalOutput").ap()
        k_dma("sO", dbg_q, q_scr, wait=[("sV", s.cnt["sV"])])
        k_dma("sO", dbg_cck, cck_out_t.ap())
        k_dma("sO", dbg_ccv, ccv_out_t.ap())
    if stage < 6:
        dbg = nc.dram_tensor("dbg_xln", [D, TPC], BF16, kind="ExternalOutput").ap()
        k_dma("sO", dbg.rearrange("(c p) t -> p c t", p=128),
              xln[:, :].rearrange("p (c t) -> p c t", t=TPC),
              wait=[("sV", s.cnt["sV"]), ("sS", s.cnt["sS"])])
        if stage >= 2:
            dbg2 = nc.dram_tensor("dbg_wd", [128, 8 * 16], BF16, kind="ExternalOutput").ap()
            k_dma("sO", dbg2, wden_all[:, :])
    s.wait("sync", [("sO", s.cnt["sO"])])

    with ctx:
        block = ctx.enter_context(nc.Block())
        for eng_name in ("tensor", "vector", "scalar", "gpsimd", "sync"):
            def runner(eng, _fns=tuple(s.streams[eng_name])):
                for fn in _fns:
                    fn(eng)
            getattr(block, eng_name)(runner)

    return nc


def _prep_inputs(inputs):
    f = np.ascontiguousarray
    x = np.asarray(inputs["L0_attn_x"])[0]
    xT = x.T
    gw = np.asarray(inputs["L0_ffn_gate_w"])
    uw = np.asarray(inputs["L0_ffn_up_w"])
    dw = np.asarray(inputs["L0_ffn_down_w"])
    sg = np.asarray(inputs["L0_ffn_shared_gate_w"])
    su = np.asarray(inputs["L0_ffn_shared_up_w"])
    sd = np.asarray(inputs["L0_ffn_shared_down_w"])

    gw_t = gw.transpose(0, 2, 1)
    uw_t = uw.transpose(0, 2, 1)
    dw_t = dw.transpose(0, 2, 1)
    sg_seg = sg.reshape(4, 1408, 2048).transpose(0, 2, 1)
    su_seg = su.reshape(4, 1408, 2048).transpose(0, 2, 1)
    sd_seg = sd.T.reshape(4, 1408, 2048)

    def slab_major(wt):
        E = wt.shape[0]
        return wt.reshape(E, 16, 128, 11, 128).transpose(0, 3, 2, 1, 4)

    def dw_major(dt_):
        E = dt_.shape[0]
        return dt_.reshape(E, 11, 128, 2, 1024).transpose(0, 3, 2, 1, 4)

    def qkv_major(wt):
        return wt.reshape(16, 16, 128, 128).transpose(0, 2, 1, 3)

    common = {
        "wq_t": f(qkv_major(np.asarray(inputs["L0_attn_W_Q"]).transpose(0, 2, 1))),
        "wk_t": f(qkv_major(np.asarray(inputs["L0_attn_W_K"]).transpose(0, 2, 1))),
        "wv_t": f(qkv_major(np.asarray(inputs["L0_attn_W_V"]).transpose(0, 2, 1))),
        "wo_t": f(np.asarray(inputs["L0_attn_W_O"]).T),
        "router_t": f(np.asarray(inputs["L0_ffn_router_w"]).T),
        "se_t": f(np.asarray(inputs["L0_ffn_se_gate_w"]).T),
        "gw_all": f(slab_major(np.concatenate([gw_t, sg_seg], 0))),
        "uw_all": f(slab_major(np.concatenate([uw_t, su_seg], 0))),
        "dw_all": f(dw_major(np.concatenate([dw_t, sd_seg], 0))),
    }
    p = np.arange(128)[:, None]
    q = np.arange(128)[None, :]
    in_maps = []
    for c in range(R):
        masks = np.zeros((R + 1, 128, 128), np.float32)
        for r in range(R):
            masks[r] = (p <= q) if r <= c else (p < q)
        masks[R] = np.eye(128, dtype=np.float32)
        m = dict(common)
        m["x_fm"] = f(xT[:, c::8])
        m["masks"] = masks
        in_maps.append(m)
    return in_maps


def kernel(**inputs):
    in_maps = _prep_inputs(inputs)
    nc = build_program(STAGE)
    res = run_bass_kernel_spmd(nc, in_maps, core_ids=list(range(R)))
    out = np.empty((8192, 2048), np.float32)
    for c in range(R):
        out[c::8, :] = np.asarray(res.results[c]["out"]).T
    return out.reshape(1, 8192, 2048)


# version 08 (final)
# speedup vs baseline: 1.0088x; 1.0088x over previous
import numpy as np
from contextlib import ExitStack

import concourse.bass as bass
import concourse.mybir as mybir
from concourse.bass_utils import run_bass_kernel_spmd

F32 = mybir.dt.float32
BF16 = mybir.dt.bfloat16
AF = mybir.ActivationFunctionType
ALU = mybir.AluOpType
AX = mybir.AxisListType

STAGE = 6

D = 2048
DC = 16
TPC = 1024
TB = 512
NTB = 2
NE = 20
I_E = 1408
IC = 11
NH = 16
DH = 128
R = 8
SCALE = float(1.0 / np.sqrt(np.float32(DH)))

KV_RANK_ELEMS = 2 * D * TPC
K_OFF = 0
V_OFF = D * TPC


class Sched:

    def __init__(self, sems):
        self.streams = {k: [] for k in ("tensor", "vector", "scalar", "gpsimd", "sync")}
        self.cnt = {k: 0 for k in sems}
        self.sems = sems
        self.last_wait = {e: {} for e in self.streams}

    def op(self, eng, fn, wait=(), inc=None, inc_n=1):
        lw = self.last_wait[eng]
        w = []
        for sm, v in wait:
            if v > 0 and lw.get(sm, 0) < v:
                w.append((sm, v))
                lw[sm] = v
        ret = None
        if inc is not None:
            self.cnt[inc] += inc_n
            ret = self.cnt[inc]
        sems = self.sems
        wt = tuple(w)

        def thunk(e, _fn=fn, _wt=wt, _inc=inc, _n=inc_n):
            for sm, v in _wt:
                e.wait_ge(sems[sm], v)
            ins = _fn(e)
            if _inc is not None:
                ins.then_inc(sems[_inc], _n)
        self.streams[eng].append(thunk)
        return ret

    def wait(self, eng, wait):
        lw = self.last_wait[eng]
        sems = self.sems
        for sm, v in wait:
            if v > 0 and lw.get(sm, 0) < v:
                lw[sm] = v

                def thunk(e, _s=sm, _v=v):
                    e.wait_ge(sems[_s], _v)
                self.streams[eng].append(thunk)


def build_program(stage=STAGE):
    nc = bass.Bass(target_bir_lowering=False, debug=False,
                   detect_race_conditions=False)

    def din(name, shape):
        return nc.dram_tensor(name, shape, F32, kind="ExternalInput").ap()

    x_in = din("x_fm", [D, TPC])
    wq_in = din("wq_t", [NH, 128, DC * 128])
    wk_in = din("wk_t", [NH, 128, DC * 128])
    wv_in = din("wv_t", [NH, 128, DC * 128])
    wo_in = din("wo_t", [D, D])
    rt_in = din("router_t", [D, 16])
    se_in = din("se_t", [D, 1])
    gw_in = din("gw_all", [NE, IC, 128, DC * 128])
    uw_in = din("uw_all", [NE, IC, 128, DC * 128])
    dw_in = din("dw_all", [NE, 2, 128, IC * 1024])
    mk_in = din("masks", [R + 1, 128, 128])

    out_t = nc.dram_tensor("out", [D, TPC], F32, kind="ExternalOutput")
    out_ap = out_t.ap()

    q_scr = nc.dram_tensor("q_scr", [D, TPC], BF16).ap()
    NKV = D * TPC
    cck_in_t = nc.dram_tensor("cck_in", [NKV], BF16)
    ccv_in_t = nc.dram_tensor("ccv_in", [NKV], BF16)
    cck_out_t = nc.dram_tensor("cck_out", [R * NKV], BF16, addr_space="Shared")
    ccv_out_t = nc.dram_tensor("ccv_out", [R * NKV], BF16, addr_space="Shared")

    ccin_k = cck_in_t.ap().rearrange("(a b) -> a b", b=TPC)
    ccin_v4 = ccv_in_t.ap().rearrange("(h p a k) -> h p a k", h=NH, p=128, a=8)

    def rank_k(r):
        return cck_out_t.ap()[r * NKV:(r + 1) * NKV].rearrange("(a b) -> a b", b=TPC)

    def rank_v(r):
        return ccv_out_t.ap()[r * NKV:(r + 1) * NKV].rearrange(
            "(h p a k) -> h p a k", h=NH, p=128, a=8)

    RSZ = 211200
    rsv = nc.alloc_sbuf_tensor("rsv_manual", [128, RSZ], mybir.dt.uint8)
    SB = nc.lookup_mloc(rsv).addr
    off = SB

    def at(name, shape, dtype, o):
        return nc.alloc_sbuf_tensor_at(name, shape, dtype, offset=o)

    x_res = at("x_res", [128, DC * TPC], F32, off); off += 65536
    xln = at("xln", [128, DC * TPC], BF16, off); off += 32768
    dwA = at("dwA", [128, IC * 1024], BF16, off); off += 22528
    dwB = at("dwB", [128, IC * 1024], BF16, off); off += 22528
    gwb = [at(f"gw{j}", [128, DC * 128], BF16, off + 4096 * j) for j in range(2)]
    off += 8192
    uwb = [at(f"uw{j}", [128, DC * 128], BF16, off + 4096 * j) for j in range(2)]
    Pxt = [nc.alloc_sbuf_tensor_at(f"Pst{j+2}", [128, TB], BF16, offset=off + 4096 * j + 2048)
           for j in range(2)]
    off += 8192

    U = off; off += 32768
    Hb = [at("H0", [128, IC * TB], BF16, U),
          at("H1", [128, IC * TB], BF16, U + 11264)]
    silu_sb = [at(f"silu{j}", [128, TB], BF16, U + 22528 + 1024 * j) for j in range(2)]
    tmp_sb = [at(f"tmp{j}", [128, TB], BF16, U + 24576 + 1024 * j) for j in range(2)]
    wqb = [at(f"wqv{j}", [128, DC * 128], BF16, U + 4096 * j) for j in range(2)]
    wkb = [at(f"wkv{j}", [128, DC * 128], BF16, U + 8192 + 4096 * j) for j in range(2)]
    wvb = [at(f"wvv{j}", [128, DC * 128], BF16, U + 16384 + 4096 * j) for j in range(2)]
    vstage = at("vstage", [128, 128], BF16, U + 24576)
    kqstage = [at(f"kqs{j}", [128, TB], BF16, U + 24832 + 1024 * j) for j in range(4)]
    z_qb = at("z_qb", [128, NH * TB], BF16, U)
    Kst = [at(f"Kst{j}", [128, TPC], BF16, U + 16384 + 2048 * j) for j in range(2)]
    Vst = [at(f"Vst{j}", [128, 8 * 128], BF16, U + 20480 + 2048 * j) for j in range(2)]
    Pst = [at(f"Pst{j}", [128, TB], BF16, U + 24576 + 1024 * j) for j in range(2)]
    Qt = [at(f"Qt{j}", [128, TB], BF16, U + 26624 + 1024 * j) for j in range(2)]
    wob = [at(f"wob{j}", [128, TB], BF16, U + 28672 + 1024 * j) for j in range(2)]
    linv_b = at("linv_b", [128, TB], F32, U + 30720)

    USP = U + 26624
    wden_all = at("wden_all", [128, 8 * 16], BF16, USP)
    wrow_bf = at("wrow_bf", [1, TB], BF16, USP + 256)
    wb_sb = [at(f"wb{j}", [128, TB], BF16, USP + 1280 + 1024 * j) for j in range(2)]
    sig_sb = [at(f"sig{j}", [128, TB], BF16, USP + 3328 + 1024 * j) for j in range(2)]
    probs = at("probs", [128, 16], F32, USP + 5376)
    maxes = at("maxes", [128, 8], F32, USP + 5440)
    wmask = at("wmask", [128, 16], F32, USP + 5472)

    ab_a = at("ab_a", [128, TB], F32, off); off += 2048
    ab_b = at("ab_b", [128, TB], F32, off); off += 2048
    sq_sb = [at(f"sq{j}", [128, TB], BF16, off + 1024 * j) for j in range(2)]; off += 2048
    masks_bf = at("masks_bf", [128, (R + 1) * 128], BF16, off); off += 2304
    router_bf = at("router_bf", [128, DC * 16], BF16, off); off += 512
    se_bf = at("se_bf", [128, DC], BF16, off); off += 32
    ones_col_bf = at("ones_col_bf", [128, 1], BF16, off); off += 32
    ones_row_bf = at("ones_row_bf", [1, 128], BF16, off); off += 256
    ones_row_f = at("ones_row_f", [1, 128], F32, off); off += 512
    row_m_t = at("row_m", [1, TB], F32, off); off += 2048
    row_m = row_m_t[0:1, :]
    row_a = at("row_a", [1, TB], F32, off); off += 2048
    row_b = at("row_b", [1, TB], F32, off); off += 2048
    row_l_off = off
    row_l = at("row_l", [1, TB], F32, off); off += 2048
    row_sg = at("row_sg", [1, TB], BF16, row_l_off)
    row_s_t = at("row_s", [1, TB], F32, row_l_off)
    row_s = row_s_t[0:1, :]
    rmax_neg = at("rmax_neg", [128, 1], F32, off); off += 32
    rsum = at("rsum", [128, 1], F32, off); off += 32
    rsum2 = at("rsum2", [128, 1], F32, off); off += 32
    assert off <= SB + RSZ, f"SBUF overflow: {off} > {SB + RSZ}"

    ctx = ExitStack()
    ps = [ctx.enter_context(nc.psum_tensor(f"ps{j}", [128, TB], F32)) for j in range(8)]

    semnames = ["sX", "sI", "sT", "sV", "sS", "sC", "sO",
                "sG0", "sG1", "sD0", "sD1", "sQW0", "sQW1",
                "sWo0", "sWo1", "sK0", "sK1", "sV0", "sV1", "sQ0", "sQ1",
                "sE0", "sE1", "sE2", "sE3", "sEV"]
    sems = {n: ctx.enter_context(nc.semaphore(n)) for n in semnames}
    s = Sched(sems)

    def t_mm(out, lhsT, rhs, start, stop, wait=(), inc=None):
        return s.op("tensor",
                    lambda e, o=out, l=lhsT, r=rhs, st=start, sp=stop:
                    e.matmul(o, l, r, start=st, stop=sp, skip_group_check=True),
                    wait=wait, inc=inc)

    bank_free = [("sT", 0)] * 8

    def chain(bank, pairs, wait=(), out=None, inc="sT"):
        o = ps[bank][:, :] if out is None else out
        n = len(pairs)
        w = list(wait) + [bank_free[bank]]
        cnt = None
        for j, (l, r) in enumerate(pairs):
            cnt = t_mm(o, l, r, j == 0, j == n - 1,
                       wait=w if j == 0 else (), inc=inc if j == n - 1 else None)
        return cnt

    def dma(eng, sem, dst, src, wait=(), slow=False):
        cnt = s.op(eng, lambda e, d=dst, sr=src, sl=slow: e.dma_start(
            out=d, in_=sr, allow_slow_non_contiguous=sl),
            wait=wait, inc=sem, inc_n=16)
        return (sem, cnt)

    def g_dma(sem, dst, src, wait=(), slow=False):
        return dma("gpsimd", sem, dst, src, wait, slow)

    def a_dma(sem, dst, src, wait=()):
        return dma("scalar", sem, dst, src, wait)

    def k_dma(sem, dst, src, wait=()):
        return dma("sync", sem, dst, src, wait)

    x_ld = k_dma("sX", x_res[:, :].rearrange("p (c t) -> p c t", t=TPC),
                 x_in.rearrange("(c p) t -> p c t", p=128))
    g_dma("sI", masks_bf[:, :].rearrange("p (r q) -> p r q", q=128),
          mk_in.rearrange("r p q -> p r q"))
    g_dma("sI", router_bf[:, :].rearrange("p (c e) -> p c e", e=16),
          rt_in.rearrange("(c p) e -> p c e", p=128))
    init_ld = g_dma("sI", se_bf[:, :], se_in.rearrange("(c p) e -> p (c e)", p=128),
                    slow=True)
    mk_ld = rt_ld = se_ld = init_ld

    s.op("vector", lambda e: e.memset(ones_col_bf[:, :], 1.0))
    s.op("vector", lambda e: e.memset(ones_row_bf[:, :], 1.0))
    s.op("vector", lambda e: e.memset(ones_row_f[:, :], 1.0))
    const_cnt = s.op("vector", lambda e: e.memset(rmax_neg[:, :], 0.0), inc="sV")

    xln_ready = {0: 0, 1: 0}
    st = {
        "slab_free": {0: 0, 1: 0},
        "dw_free": {0: 0, 1: 0},
        "H_free": {0: 0, 1: 0},
        "wqkv_free": {0: 0, 1: 0},
        "K_free": {0: 0, 1: 0},
        "V_free": {0: 0, 1: 0},
        "Qt_free": {0: 0, 1: 0},
        "wo_free": {0: 0, 1: 0},
        "P_free": {j: 0 for j in range(4)},
        "kq_free": {j: (f"sE{j}", 0) for j in range(4)},
        "vstage_free": ("sEV", 0),
        "sig_ready": {0: 0, 1: 0},
        "wdT_ready": 0,
        "wb_ready": {0: 0, 1: 0},
        "row_a_free": 0,
        "row_b_free": 0,
        "row_sg_free": 0,
        "row_l_free": 0,
        "row_s_free": 0,
    }

    def xs(c, toff, n=TB):
        return xln[:, c * TPC + toff: c * TPC + toff + n]

    def emit_ln(tb, x_wait):
        tb_off = tb * TB
        cast_cnt = []
        for c in range(DC):
            w = list(x_wait) if c == 0 else ()
            cc = s.op("scalar",
                      lambda e, c=c: e.copy(out=xs(c, tb_off), in_=x_res[:, c * TPC + tb_off: c * TPC + tb_off + TB]),
                      wait=w, inc="sS")
            cast_cnt.append(cc)

        sq_cnt = [0] * DC
        sqmm_cnt = [0] * DC
        for c in range(DC):
            w = [("sS", cast_cnt[c])]
            if c >= 2:
                w.append(("sT", sqmm_cnt[c - 2]))
            sq_cnt[c] = s.op("vector",
                             lambda e, c=c: e.tensor_mul(out=sq_sb[c % 2][:, :],
                                                         in0=xs(c, tb_off), in1=xs(c, tb_off)),
                             wait=w, inc="sV")
            w = [("sS", cast_cnt[c]), ("sV", const_cnt)]
            if c == 0:
                w += [bank_free[6], bank_free[7]]
            t_mm(ps[6][0:1, :], ones_col_bf[:, :], xs(c, tb_off),
                 c == 0, c == DC - 1, wait=w, inc="sT" if c == DC - 1 else None)
            sqmm_cnt[c] = t_mm(ps[7][0:1, :], ones_col_bf[:, :], sq_sb[c % 2][:, :],
                               c == 0, c == DC - 1,
                               wait=[("sV", sq_cnt[c])], inc="sT")
        stats_mm = sqmm_cnt[DC - 1]

        s.op("vector", lambda e: e.tensor_scalar_mul(row_m, ps[6][0:1, :], 1.0 / D),
             wait=[("sT", stats_mm)])
        s.op("vector", lambda e: e.tensor_scalar_mul(row_s, ps[7][0:1, :], 1.0 / D),
             wait=[("sT", st["row_l_free"]), ("sT", st["row_sg_free"])])
        s.op("vector", lambda e: e.tensor_mul(row_a[:, :], row_m, row_m),
             wait=[("sT", st["row_a_free"])])
        s.op("vector", lambda e: e.tensor_sub(row_s, row_s, row_a[:, :]))
        vcnt = s.op("vector", lambda e: e.tensor_scalar_add(row_s, row_s, 1e-5),
                    inc="sV")
        bank_free[6] = ("sV", vcnt)
        bank_free[7] = ("sV", vcnt)
        s.op("scalar", lambda e: e.activation(out=row_b[:, :], in_=row_s,
                                              func=AF.Ln),
             wait=[("sV", vcnt), ("sT", st["row_b_free"])])
        scnt = s.op("scalar", lambda e: e.activation(out=row_a[:, :], in_=row_b[:, :],
                                                     func=AF.Exp, scale=-0.5),
                    wait=[("sT", st["row_a_free"])], inc="sS")
        st["row_s_free"] = scnt
        s.op("vector", lambda e: e.tensor_mul(row_b[:, :], row_m, row_a[:, :]),
             wait=[("sS", scnt)])
        bcnt = s.op("vector", lambda e: e.tensor_scalar_mul(row_b[:, :], row_b[:, :], -1.0),
                    inc="sV")
        am = t_mm(ps[4][:, :], ones_row_f[:, :], row_a[:, :], True, True,
                  wait=[("sS", scnt), bank_free[4]], inc="sT")
        bm = t_mm(ps[5][:, :], ones_row_f[:, :], row_b[:, :], True, True,
                  wait=[("sV", bcnt), bank_free[5]], inc="sT")
        st["row_a_free"] = am
        st["row_b_free"] = bm
        ea = s.op("vector", lambda e: e.tensor_copy(out=ab_a[:, :], in_=ps[4][:, :]),
                  wait=[("sT", am)], inc="sV")
        eb = s.op("vector", lambda e: e.tensor_copy(out=ab_b[:, :], in_=ps[5][:, :]),
                  wait=[("sT", bm)], inc="sV")
        bank_free[4] = ("sV", ea)
        bank_free[5] = ("sV", eb)
        last = 0
        for c in range(DC):
            w = [("sV", eb), bank_free[3]] if c == 0 else ()
            s.op("vector", lambda e, c=c: e.tensor_mul(
                ps[3][:, :], x_res[:, c * TPC + tb_off: c * TPC + tb_off + TB], ab_a[:, :]),
                wait=w)
            last = s.op("vector", lambda e, c=c: e.tensor_add(
                xs(c, tb_off), ps[3][:, :], ab_b[:, :]), inc="sV")
        bank_free[3] = ("sV", last)
        xln_ready[tb] = last

    def emit_routing():
        exp_cnt = 0
        wdc = 0
        for tc in range(8):
            tb = tc // 4
            col = tc * 128
            w = [("sV", xln_ready[1]), rt_ld, bank_free[4]]
            if tc > 0:
                w.append(("sS", exp_cnt))
            mmc = None
            for j in range(DC):
                mmc = t_mm(ps[4][:, 0:16], xs(j, col, 128), router_bf[:, j * 16:(j + 1) * 16],
                           j == 0, j == DC - 1, wait=w if j == 0 else (),
                           inc="sT" if j == DC - 1 else None)
            rm = s.op("vector", lambda e: e.reduce_max(rmax_neg[:, :], ps[4][:, 0:16],
                                                       axis=AX.X, negate=True),
                      wait=[("sT", mmc)], inc="sV")
            eu = s.op("scalar", lambda e: e.activation(out=probs[:, :], in_=ps[4][:, 0:16],
                                                       func=AF.Exp, bias=rmax_neg[:, 0:1]),
                      wait=[("sV", rm)], inc="sS")
            rs = s.op("vector", lambda e: e.reduce_sum(rsum[:, :], probs[:, :], axis=AX.X),
                      wait=[("sS", eu)], inc="sV")
            s.op("scalar", lambda e: e.activation(out=rsum2[:, :], in_=rsum[:, :],
                                                  func=AF.Ln), wait=[("sV", rs)])
            s.op("scalar", lambda e: e.activation(out=rsum2[:, :], in_=rsum2[:, :],
                                                  func=AF.Exp, scale=-1.0))
            exp_cnt = s.op("scalar", lambda e: e.activation(out=probs[:, :], in_=probs[:, :],
                                                            func=AF.Identity,
                                                            scale=rsum2[:, 0:1]),
                           inc="sS")
            c3 = s.op("vector", lambda e: e.max(out=maxes[:, :], in_=probs[:, :]),
                      wait=[("sS", exp_cnt)], inc="sV")
            c4 = s.op("vector", lambda e: e.tensor_scalar(out=wmask[:, :], in0=probs[:, :],
                                                          scalar1=maxes[:, 3:4], scalar2=None,
                                                          op0=ALU.is_ge),
                      wait=[("sV", c3)], inc="sV")
            c5 = s.op("vector", lambda e: e.tensor_mul(probs[:, :], probs[:, :], wmask[:, :]),
                      wait=[("sV", c4)], inc="sV")
            s.wait("vector", [("sV", c5)])
            wdc = s.op("vector", lambda e, tc=tc: e.tensor_copy(
                out=wden_all[:, tc * 16:(tc + 1) * 16], in_=probs[:, :]),
                wait=[("sT", s.cnt["sT"])] if tc == 0 else (), inc="sV")
        bank_free[4] = ("sS", exp_cnt)
        st["wdT_ready"] = wdc

        for tb in range(NTB):
            tb_off = tb * TB
            pairs = [(se_bf[:, c:c + 1], xs(c, tb_off)) for c in range(DC)]
            mmc = chain(6, pairs, wait=[("sV", xln_ready[tb]), se_ld],
                        out=ps[6][0:1, :])
            sg = s.op("scalar", lambda e: e.activation(out=row_sg[:, :], in_=ps[6][0:1, :],
                                                       func=AF.Sigmoid),
                      wait=[("sT", mmc), ("sT", st["row_sg_free"])], inc="sS")
            bank_free[6] = ("sS", sg)
            bm = t_mm(ps[5][:, :], ones_row_bf[:, :], row_sg[:, :], True, True,
                      wait=[("sS", sg), bank_free[5]], inc="sT")
            st["row_sg_free"] = bm
            sv = s.op("vector", lambda e, tb=tb: e.tensor_copy(out=sig_sb[tb][:, :],
                                                               in_=ps[5][:, :]),
                      wait=[("sT", bm)], inc="sV")
            bank_free[5] = ("sV", sv)
            st["sig_ready"][tb] = sv

    def emit_expert(e_idx):
        moe = e_idx < 16
        h_free_prev = dict(st["H_free"])

        slab_ld = [0] * IC

        def load_slab(i, free_cnt):
            par = i % 2
            sem = f"sG{par}"
            g_dma(sem, gwb[par][:, :], gw_in[e_idx, i], wait=[("sT", free_cnt)])
            slab_ld[i] = g_dma(sem, uwb[par][:, :], uw_in[e_idx, i])

        load_slab(0, st["slab_free"][0])
        load_slab(1, st["slab_free"][1])
        dwa_ld = g_dma("sD0", dwA[:, :], dw_in[e_idx, 0],
                       wait=[("sT", st["dw_free"][0])])
        dwb_ld = g_dma("sD1", dwB[:, :], dw_in[e_idx, 1],
                       wait=[("sT", st["dw_free"][1])])

        if moe:
            ident = masks_bf[:, R * 128:(R + 1) * 128]
            for tb in range(NTB):
                tc0 = 4 * tb
                tmc = 0
                for j in range(4):
                    w = [("sV", st["wdT_ready"]), mk_ld, bank_free[5]] if j == 0 else ()
                    tmc = t_mm(ps[5][0:1, 128 * j:128 * (j + 1)],
                               wden_all[:, (tc0 + j) * 16 + e_idx:(tc0 + j) * 16 + e_idx + 1],
                               ident, True, True, wait=w,
                               inc="sT" if j == 3 else None)
                re_ = s.op("vector", lambda e: e.tensor_copy(out=wrow_bf[:, :],
                                                             in_=ps[5][0:1, :]),
                           wait=[("sT", tmc)], inc="sV")
                bm = t_mm(ps[5][:, :], ones_row_bf[:, :], wrow_bf[:, :],
                          True, True, wait=[("sV", re_)], inc="sT")
                wv = s.op("vector", lambda e, tb=tb: e.tensor_copy(out=wb_sb[tb][:, :],
                                                                   in_=ps[5][:, :]),
                          wait=[("sT", bm)], inc="sV")
                bank_free[5] = ("sV", wv)
                st["wb_ready"][tb] = wv
            wb_use = wb_sb
            wb_wait = dict(st["wb_ready"])
        else:
            wb_use = sig_sb
            wb_wait = dict(st["sig_ready"])

        h_cnt = {0: 0, 1: 0}
        first_h = {0: True, 1: True}
        for i in range(IC):
            par = i % 2
            wslab = [slab_ld[i]]
            for tb in range(NTB):
                gb, ub = (0, 1) if tb == 0 else (2, 3)
                pairs_g = [(gwb[par][:, 128 * c:128 * (c + 1)], xs(c, tb * TB))
                           for c in range(DC)]
                gc = chain(gb, pairs_g, wait=wslab + [("sV", xln_ready[tb])])
                pairs_u = [(uwb[par][:, 128 * c:128 * (c + 1)], xs(c, tb * TB))
                           for c in range(DC)]
                uc = chain(ub, pairs_u, wait=wslab)
                if tb == NTB - 1:
                    st["slab_free"][par] = uc
                sl = s.op("scalar", lambda e, par=par, gb=gb: e.activation(
                    out=silu_sb[par][:, :], in_=ps[gb][:, :], func=AF.Silu),
                    wait=[("sT", gc)], inc="sS")
                bank_free[gb] = ("sS", sl)
                tm = s.op("vector", lambda e, par=par, ub=ub, tb=tb: e.tensor_mul(
                    out=tmp_sb[par][:, :], in0=ps[ub][:, :], in1=wb_use[tb][:, :]),
                    wait=[("sT", uc), ("sV", wb_wait[tb])], inc="sV")
                bank_free[ub] = ("sV", tm)
                hw = [("sS", sl)]
                if first_h[tb]:
                    hw.append(("sT", h_free_prev[tb]))
                    first_h[tb] = False
                h_cnt[tb] = s.op("vector", lambda e, par=par, tb=tb, i=i: e.tensor_mul(
                    out=Hb[tb][:, i * TB:(i + 1) * TB],
                    in0=silu_sb[par][:, :], in1=tmp_sb[par][:, :]),
                    wait=hw, inc="sV")
            if i + 2 < IC:
                load_slab(i + 2, st["slab_free"][(i + 2) % 2])

        for tb in range(NTB):
            for dout in range(DC):
                bank = dout % 2
                dwbuf, dw_ld, half = (dwA, dwa_ld, 0) if dout < 8 else (dwB, dwb_ld, 1)
                dcol = 128 * (dout - 8 * half)
                pairs = [(dwbuf[:, c * 1024 + dcol: c * 1024 + dcol + 128],
                          Hb[tb][:, c * TB:(c + 1) * TB]) for c in range(IC)]
                dc = chain(bank, pairs, wait=[dw_ld, ("sV", h_cnt[tb])])
                if dout == 7:
                    if tb == NTB - 1:
                        st["dw_free"][0] = dc
                elif dout == DC - 1:
                    st["H_free"][tb] = dc
                    if tb == NTB - 1:
                        st["dw_free"][1] = dc
                ev = s.op("vector", lambda e, tb=tb, dout=dout, bank=bank:
                          e.scalar_tensor_tensor(
                              out=x_res[:, dout * TPC + tb * TB: dout * TPC + tb * TB + TB],
                              in0=ps[bank][:, :], scalar=0.5,
                              in1=x_res[:, dout * TPC + tb * TB: dout * TPC + tb * TB + TB],
                              op0=ALU.mult, op1=ALU.add),
                          wait=[("sT", dc)], inc="sV")
                bank_free[bank] = ("sV", ev)

    def emit_qkv():
        u_free = [("sT", s.cnt["sT"]), ("sV", s.cnt["sV"])]
        kq_slot = [0]

        def export(psbank, mmc, dst):
            sl = kq_slot[0] % 4
            kq_slot[0] += 1
            ev = s.op("vector", lambda e, b=psbank, sl=sl: e.tensor_copy(
                out=kqstage[sl][:, :], in_=ps[b][:, :]),
                wait=[("sT", mmc), st["kq_free"][sl]], inc="sV")
            bank_free[psbank] = ("sV", ev)
            st["kq_free"][sl] = a_dma(f"sE{sl}", dst, kqstage[sl][:, :],
                                      wait=[("sV", ev)])

        for h in range(NH):
            par = h % 2
            sem = f"sQW{par}"
            w = [("sT", st["wqkv_free"][par])]
            if h == 0:
                w += u_free
            wld = g_dma(sem, wkb[par][:, :], wk_in[h], wait=w)
            last_mm = 0
            for tb in range(NTB):
                kb = tb
                pairs = [(wkb[par][:, 128 * c:128 * (c + 1)], xs(c, tb * TB))
                         for c in range(DC)]
                kc = chain(kb, pairs, wait=[wld, ("sV", xln_ready[tb])])
                last_mm = kc
                export(kb, kc, ccin_k[128 * h:128 * (h + 1), tb * TB:tb * TB + TB])
            st["wqkv_free"][par] = last_mm
        k_exp_waits = [(f"sE{j}", s.cnt[f"sE{j}"]) for j in range(4)]
        cck_cnt = s.op("gpsimd",
                       lambda e: e.collective_compute(
                           "AllGather", ALU.bypass,
                           replica_groups=[list(range(R))],
                           ins=[cck_in_t.ap().opt()],
                           outs=[cck_out_t.ap().opt()]),
                       wait=k_exp_waits, inc="sC")

        for h in range(NH):
            par = h % 2
            wld = g_dma(f"sQW{par}", wvb[par][:, :], wv_in[h],
                        wait=[("sT", st["wqkv_free"][par])])
            last_mm = 0
            for tc in range(8):
                vb = 4 + (tc % 2)
                toff = tc * 128
                pairs = [(xs(c, toff, 128), wvb[par][:, 128 * c:128 * (c + 1)])
                         for c in range(DC)]
                vc = chain(vb, pairs, wait=[wld], out=ps[vb][:, 0:128])
                last_mm = vc
                ev = s.op("vector", lambda e, vb=vb: e.tensor_copy(
                    out=vstage[:, :], in_=ps[vb][:, 0:128]),
                    wait=[("sT", vc), st["vstage_free"]], inc="sV")
                bank_free[vb] = ("sV", ev)
                st["vstage_free"] = a_dma(
                    "sEV", ccin_v4[h, :, tc, :], vstage[:, :], wait=[("sV", ev)])
            st["wqkv_free"][par] = last_mm
        ccv_cnt = s.op("gpsimd",
                       lambda e: e.collective_compute(
                           "AllGather", ALU.bypass,
                           replica_groups=[list(range(R))],
                           ins=[ccv_in_t.ap().opt()],
                           outs=[ccv_out_t.ap().opt()]),
                       wait=[("sEV", s.cnt["sEV"])], inc="sC")

        for h in range(NH):
            par = h % 2
            wld = g_dma(f"sQW{par}", wqb[par][:, :], wq_in[h],
                        wait=[("sT", st["wqkv_free"][par])])
            last_mm = 0
            for tb in range(NTB):
                qb_ = 2 + tb
                pairs = [(wqb[par][:, 128 * c:128 * (c + 1)], xs(c, tb * TB))
                         for c in range(DC)]
                qc = chain(qb_, pairs, wait=[wld])
                last_mm = qc
                export(qb_, qc, q_scr[128 * h:128 * (h + 1), tb * TB:tb * TB + TB])
            st["wqkv_free"][par] = last_mm
        st["export_done"] = [(f"sE{j}", s.cnt[f"sE{j}"]) for j in range(4)]
        return cck_cnt, ccv_cnt

    def emit_flash(qb, cc_cnt):
        cck_cnt, ccv_cnt = cc_cnt
        if qb == 0:
            tcur = s.cnt["sT"]
            for j in range(2):
                st["K_free"][j] = max(st["K_free"][j], tcur)
                st["V_free"][j] = max(st["V_free"][j], tcur)
                st["Qt_free"][j] = max(st["Qt_free"][j], tcur)
                st["wo_free"][j] = max(st["wo_free"][j], tcur)
        amax = 4 * qb + 4
        SBANKS = [0, 1, 3, 4]
        PD = 4
        Pfull = Pst + Pxt
        exp_free = {j: 0 for j in range(PD)}
        first_s = {j: True for j in range(PD)}
        for h in range(NH):
            qpar = h % 2
            qld = a_dma(f"sQ{qpar}", Qt[qpar][:, :],
                        q_scr[128 * h:128 * (h + 1), qb * TB:qb * TB + TB],
                        wait=[("sT", st["Qt_free"][qpar])])
            lastS = 0
            lastAV = 0
            tiles = [(r, a) for r in range(R) for a in range(amax)]
            nt = len(tiles)
            kvld = {}
            pend = {}
            AHEAD = 2

            def emit_S(j):
                nonlocal lastS
                r, a = tiles[j]
                rpar = r % 2
                if a == 0:
                    kld = k_dma(f"sK{rpar}", Kst[rpar][:, 0:amax * 128],
                                rank_k(r)[128 * h:128 * (h + 1), 0:amax * 128],
                                wait=[("sT", st["K_free"][rpar]), ("sC", cck_cnt)])
                    vld = k_dma(
                        f"sV{rpar}",
                        Vst[rpar][:, 0:amax * 128].rearrange("p (a k) -> p a k", k=128),
                        rank_v(r)[h, :, 0:amax, :],
                        wait=[("sT", st["V_free"][rpar]), ("sC", ccv_cnt)])
                    kvld[r] = (kld, vld)
                kld, vld = kvld[r]
                spar = j % PD
                sbank = SBANKS[spar]
                jc = a - 4 * qb
                c0 = 128 * jc if jc > 0 else 0
                w = [kld, qld, ("sS", exp_free[spar])]
                if first_s[spar]:
                    w.append(bank_free[sbank])
                    first_s[spar] = False
                smm = t_mm(ps[sbank][:, c0:TB], Kst[rpar][:, 128 * a:128 * (a + 1)],
                           Qt[qpar][:, c0:TB], True, True, wait=w, inc="sT")
                lastS = smm
                ex = s.op("scalar", lambda e, spar=spar, sbank=sbank, c0=c0: e.activation(
                    out=Pfull[spar][:, c0:TB], in_=ps[sbank][:, c0:TB], func=AF.Exp,
                    scale=SCALE),
                    wait=[("sT", smm), ("sT", st["P_free"][spar])], inc="sS")
                exp_free[spar] = ex
                p_wait = [("sS", ex)]
                if jc >= 0:
                    mv = s.op("vector", lambda e, spar=spar, c0=c0, r=r: e.tensor_mul(
                        out=Pfull[spar][:, c0:c0 + 128],
                        in0=Pfull[spar][:, c0:c0 + 128],
                        in1=masks_bf[:, 128 * r:128 * (r + 1)]),
                        wait=[("sS", ex), mk_ld], inc="sV")
                    p_wait = [("sV", mv)]
                pend[j] = (r, a, spar, c0, p_wait, vld, smm)

            def emit_AVl(j):
                nonlocal lastAV
                r, a, spar, c0, p_wait, vld, smm = pend.pop(j)
                rpar = r % 2
                startg = (j == 0)
                stopg = (j == nt - 1)
                w2 = list(p_wait) + [vld]
                if startg:
                    w2 += [bank_free[2], bank_free[6]]
                t_mm(ps[2][:, c0:TB], Vst[rpar][:, 128 * a:128 * (a + 1)],
                     Pfull[spar][:, c0:TB], startg, stopg, wait=w2)
                lastAV = t_mm(ps[6][0:1, c0:TB], ones_col_bf[:, :],
                              Pfull[spar][:, c0:TB], startg, stopg, inc="sT")
                st["P_free"][spar] = lastAV
                if a == amax - 1:
                    st["K_free"][rpar] = smm
                    st["V_free"][rpar] = lastAV

            for j in range(nt):
                emit_S(j)
                if j >= AHEAD:
                    emit_AVl(j - AHEAD)
            for j in range(nt - AHEAD, nt):
                emit_AVl(j)
            st["Qt_free"][qpar] = lastS
            for j in range(PD):
                bank_free[SBANKS[j]] = ("sS", exp_free[j])
            s.op("scalar", lambda e: e.activation(out=row_l[:, :], in_=ps[6][0:1, :],
                                                  func=AF.Ln),
                 wait=[("sT", lastAV), ("sT", st["row_l_free"]),
                       ("sS", st["row_s_free"])])
            rc = s.op("scalar", lambda e: e.activation(out=row_l[:, :], in_=row_l[:, :],
                                                       func=AF.Exp, scale=-1.0),
                      inc="sS")
            bank_free[6] = ("sS", rc)
            bm = t_mm(ps[5][:, :], ones_row_f[:, :], row_l[:, :], True, True,
                      wait=[("sS", rc), bank_free[5]], inc="sT")
            st["row_l_free"] = bm
            lb = s.op("vector", lambda e: e.tensor_copy(out=linv_b[:, :], in_=ps[5][:, :]),
                      wait=[("sT", bm)], inc="sV")
            bank_free[5] = ("sV", lb)
            zc = s.op("vector", lambda e, h=h: e.tensor_mul(
                out=z_qb[:, h * TB:(h + 1) * TB], in0=ps[2][:, :], in1=linv_b[:, :]),
                wait=[("sV", lb)], inc="sV")
            bank_free[2] = ("sV", zc)
        z_done = s.cnt["sV"]

        for grp in range(4):
            wo_ld = [0] * NH

            def load_wo(f, free_cnt, extra=()):
                par = f % 2
                wo_ld[f] = g_dma(f"sWo{par}", wob[par][:, :],
                                 wo_in[128 * f:128 * (f + 1), grp * TB:grp * TB + TB],
                                 wait=[("sT", free_cnt)] + list(extra))

            ex0 = st["export_done"] if (qb == 0 and grp == 0) else ()
            load_wo(0, st["wo_free"][0], ex0)
            load_wo(1, st["wo_free"][1], ex0)
            ch = [0] * 4
            for f in range(NH):
                par = f % 2
                fcnt = 0
                for dd in range(4):
                    w = [wo_ld[f]]
                    if f == 0:
                        w += [bank_free[dd], ("sV", z_done)]
                    fcnt = t_mm(ps[dd][:, :], wob[par][:, 128 * dd:128 * (dd + 1)],
                                z_qb[:, f * TB:(f + 1) * TB],
                                f == 0, f == NH - 1, wait=w, inc="sT")
                    if f == NH - 1:
                        ch[dd] = fcnt
                if f + 2 < NH:
                    load_wo(f + 2, fcnt)
            st["wo_free"][0] = ch[3]
            st["wo_free"][1] = ch[3]
            for dd in range(4):
                dout = 4 * grp + dd
                ev = s.op("vector", lambda e, dout=dout, dd=dd: e.tensor_add(
                    out=x_res[:, dout * TPC + qb * TB: dout * TPC + qb * TB + TB],
                    in0=ps[dd][:, :],
                    in1=x_res[:, dout * TPC + qb * TB: dout * TPC + qb * TB + TB]),
                    wait=[("sT", ch[dd])], inc="sV")
                bank_free[dd] = ("sV", ev)

    emit_ln(0, [x_ld, ("sV", const_cnt)])
    emit_ln(1, [x_ld])
    if stage >= 2:
        emit_routing()
    if stage >= 3:
        for e_idx in range(NE):
            emit_expert(e_idx)
    ffn1_done = s.cnt["sV"]

    cc_cnt = 0
    if stage >= 4:
        emit_ln(0, [("sV", ffn1_done)])
        emit_ln(1, [("sV", ffn1_done)])
        cc_cnt = emit_qkv()
    if stage >= 5:
        emit_flash(0, cc_cnt)
        x2_q0 = s.cnt["sV"]
        if stage >= 6:
            emit_ln(0, [("sV", x2_q0)])
        emit_flash(1, cc_cnt)
        x2_q1 = s.cnt["sV"]
    if stage >= 6:
        emit_ln(1, [("sV", x2_q1)])
        emit_routing()
        st["H_free"][0] = max(st["H_free"][0], s.cnt["sT"])
        st["H_free"][1] = max(st["H_free"][1], s.cnt["sT"])
        st["slab_free"][0] = max(st["slab_free"][0], s.cnt["sT"])
        st["slab_free"][1] = max(st["slab_free"][1], s.cnt["sT"])
        for e_idx in range(NE):
            emit_expert(e_idx)

    x_done = s.cnt["sV"]
    k_dma("sO", out_ap.rearrange("(c p) t -> p c t", p=128),
          x_res[:, :].rearrange("p (c t) -> p c t", t=TPC),
          wait=[("sV", x_done)])

    if stage == 5:
        dbg_q = nc.dram_tensor("dbg_q", [D, TPC], BF16, kind="ExternalOutput").ap()
        dbg_cck = nc.dram_tensor("dbg_cck", [R * NKV], BF16, kind="ExternalOutput").ap()
        dbg_ccv = nc.dram_tensor("dbg_ccv", [R * NKV], BF16, kind="ExternalOutput").ap()
        k_dma("sO", dbg_q, q_scr, wait=[("sV", s.cnt["sV"])])
        k_dma("sO", dbg_cck, cck_out_t.ap())
        k_dma("sO", dbg_ccv, ccv_out_t.ap())
    if stage < 6:
        dbg = nc.dram_tensor("dbg_xln", [D, TPC], BF16, kind="ExternalOutput").ap()
        k_dma("sO", dbg.rearrange("(c p) t -> p c t", p=128),
              xln[:, :].rearrange("p (c t) -> p c t", t=TPC),
              wait=[("sV", s.cnt["sV"]), ("sS", s.cnt["sS"])])
        if stage >= 2:
            dbg2 = nc.dram_tensor("dbg_wd", [128, 8 * 16], BF16, kind="ExternalOutput").ap()
            k_dma("sO", dbg2, wden_all[:, :])
    s.wait("sync", [("sO", s.cnt["sO"])])

    with ctx:
        block = ctx.enter_context(nc.Block())
        for eng_name in ("tensor", "vector", "scalar", "gpsimd", "sync"):
            def runner(eng, _fns=tuple(s.streams[eng_name])):
                for fn in _fns:
                    fn(eng)
            getattr(block, eng_name)(runner)

    return nc


def _prep_inputs(inputs):
    f = np.ascontiguousarray
    x = np.asarray(inputs["L0_attn_x"])[0]
    xT = x.T
    gw = np.asarray(inputs["L0_ffn_gate_w"])
    uw = np.asarray(inputs["L0_ffn_up_w"])
    dw = np.asarray(inputs["L0_ffn_down_w"])
    sg = np.asarray(inputs["L0_ffn_shared_gate_w"])
    su = np.asarray(inputs["L0_ffn_shared_up_w"])
    sd = np.asarray(inputs["L0_ffn_shared_down_w"])

    gw_t = gw.transpose(0, 2, 1)
    uw_t = uw.transpose(0, 2, 1)
    dw_t = dw.transpose(0, 2, 1)
    sg_seg = sg.reshape(4, 1408, 2048).transpose(0, 2, 1)
    su_seg = su.reshape(4, 1408, 2048).transpose(0, 2, 1)
    sd_seg = sd.T.reshape(4, 1408, 2048)

    def slab_major(wt):
        E = wt.shape[0]
        return wt.reshape(E, 16, 128, 11, 128).transpose(0, 3, 2, 1, 4).reshape(E, 11, 128, 2048)

    def dw_major(dt_):
        E = dt_.shape[0]
        return dt_.reshape(E, 11, 128, 2, 1024).transpose(0, 3, 2, 1, 4).reshape(E, 2, 128, 11264)

    def qkv_major(wt):
        return wt.reshape(16, 16, 128, 128).transpose(0, 2, 1, 3).reshape(16, 128, 2048)

    common = {
        "wq_t": f(qkv_major(np.asarray(inputs["L0_attn_W_Q"]).transpose(0, 2, 1))),
        "wk_t": f(qkv_major(np.asarray(inputs["L0_attn_W_K"]).transpose(0, 2, 1))),
        "wv_t": f(qkv_major(np.asarray(inputs["L0_attn_W_V"]).transpose(0, 2, 1))),
        "wo_t": f(np.asarray(inputs["L0_attn_W_O"]).T),
        "router_t": f(np.asarray(inputs["L0_ffn_router_w"]).T),
        "se_t": f(np.asarray(inputs["L0_ffn_se_gate_w"]).T),
        "gw_all": f(slab_major(np.concatenate([gw_t, sg_seg], 0))),
        "uw_all": f(slab_major(np.concatenate([uw_t, su_seg], 0))),
        "dw_all": f(dw_major(np.concatenate([dw_t, sd_seg], 0))),
    }
    p = np.arange(128)[:, None]
    q = np.arange(128)[None, :]
    in_maps = []
    for c in range(R):
        masks = np.zeros((R + 1, 128, 128), np.float32)
        for r in range(R):
            masks[r] = (p <= q) if r <= c else (p < q)
        masks[R] = np.eye(128, dtype=np.float32)
        m = dict(common)
        m["x_fm"] = f(xT[:, c::8])
        m["masks"] = masks
        in_maps.append(m)
    return in_maps


def kernel(**inputs):
    in_maps = _prep_inputs(inputs)
    nc = build_program(STAGE)
    res = run_bass_kernel_spmd(nc, in_maps, core_ids=list(range(R)))
    out = np.empty((8192, 2048), np.float32)
    for c in range(R):
        out[c::8, :] = np.asarray(res.results[c]["out"]).T
    return out.reshape(1, 8192, 2048)
